# version 06
# baseline (speedup 1.0000x reference)
import contextlib

import numpy as np
import ml_dtypes
import concourse.bass as bass
import concourse.mybir as mybir
from concourse.bass_utils import run_bass_kernel_spmd

F32 = mybir.dt.float32
BF16 = mybir.dt.bfloat16
AF = mybir.ActivationFunctionType
ALU = mybir.AluOpType
AX = mybir.AxisListType

NCORE = 8
D = 2048
DI = 512
NH = 32
NS = 16
DTR = 128
DH = 8192
T = 512
NT = 8
TA = 256
NKD = 16
NHC = 64
EPS = 1e-5

ENGS = ("tensor", "vector", "scalar", "gpsimd", "sync")

DBG_NT = None
DBG_TAPS = False


class Sched:

    def __init__(self):
        self.q = {e: [] for e in ENGS}
        self.c = {}

    def sem(self, *names):
        for n in names:
            self.c[n] = 0

    def wait(self, eng, sem, val):
        if val > 0:
            self.q[eng].append(("w", sem, int(val)))

    def waits(self, eng, pairs):
        best = {}
        for s, v in pairs:
            if v > 0:
                best[s] = max(best.get(s, 0), v)
        for s, v in best.items():
            self.q[eng].append(("w", s, int(v)))

    def do(self, eng, fn, inc=None, n=1):
        tok = None
        if inc is not None:
            self.c[inc] += n
            tok = (inc, self.c[inc])
        self.q[eng].append(("i", fn, inc, n))
        return tok


def build(sch):
    nc = bass.Bass("TRN2", target_bir_lowering=False)

    d_hs = nc.dram_tensor("hs", [NT * T, D], F32, kind="ExternalInput").ap()
    d_out = nc.dram_tensor("out", [NT * T, D], F32, kind="ExternalOutput").ap()
    d_win = nc.dram_tensor("win", [NKD, 128, 1024], F32, kind="ExternalInput").ap()
    d_wp = nc.dram_tensor("wp", [128, 4 * 1152], F32, kind="ExternalInput").ap()
    d_wdt = nc.dram_tensor("wdt", [128, 32], F32, kind="ExternalInput").ap()
    d_wo = nc.dram_tensor("wo", [4, 128, D], F32, kind="ExternalInput").ap()
    d_wg = nc.dram_tensor("wg", [NHC, 128, 2048], F32, kind="ExternalInput").ap()
    d_wu = nc.dram_tensor("wu", [NHC, 128, 2048], F32, kind="ExternalInput").ap()
    d_wd = nc.dram_tensor("wd", [NHC, 128, 2048], F32, kind="ExternalInput").ap()
    d_cw = nc.dram_tensor("cw", [128, 16], F32, kind="ExternalInput").ap()
    d_cb = nc.dram_tensor("cb", [128, 4], F32, kind="ExternalInput").ap()
    d_dd = nc.dram_tensor("dd", [128, 4], F32, kind="ExternalInput").ap()
    d_alog = nc.dram_tensor("alog", [128, 4], F32, kind="ExternalInput").ap()
    d_bdt = nc.dram_tensor("bdt", [32, 1], F32, kind="ExternalInput").ap()
    d_eb = nc.dram_tensor("eb", [32, 512], F32, kind="ExternalInput").ap()
    d_ident = nc.dram_tensor("ident", [128, 128], F32, kind="ExternalInput").ap()

    c_win = nc.dram_tensor("c_win", [NKD, 128, 1024], BF16, kind="Internal").ap()
    c_wo = nc.dram_tensor("c_wo", [4, 128, D], BF16, kind="Internal").ap()
    c_wg = nc.dram_tensor("c_wg", [NHC, 128, 2048], BF16, kind="Internal").ap()
    c_wu = nc.dram_tensor("c_wu", [NHC, 128, 2048], BF16, kind="Internal").ap()
    c_wd = nc.dram_tensor("c_wd", [NHC, 128, 2048], BF16, kind="Internal").ap()

    A = nc.alloc_sbuf_tensor
    sb_x = A("sb_x", [128, 4 * D], F32)
    sb_fm = A("sb_fm", [128, NKD * T], BF16)
    sb_gated = A("sb_gated", [128, NHC * T], BF16)
    SCRN = 4 * (T + 3) + 4 * T + 1024 + 1024
    scr = A("scr", [128, SCRN], F32)
    XC0 = 0
    XACT0 = 4 * (T + 3)
    ABAR0 = XACT0 + 4 * T
    HH0 = ABAR0 + 1024
    BU0 = 0
    CC0 = 1024
    DUMP = (0, 2048)

    xactbf = A("xactbf", [128, 4 * T], BF16)
    ybf = A("ybf", [128, 4 * T], BF16)
    dtbf = A("dtbf", [128, TA], BF16)
    sb_dv = A("sb_dv", [32, TA], F32)
    sb_da = A("sb_da", [32, TA], F32)
    sb_dm = A("sb_dm", [32, TA], F32)
    sb_dt = A("sb_dt", [32, TA], F32)
    sb_state = A("sb_state", [128, 4], F32)
    sb_carry = A("sb_carry", [128, 12], F32)
    st_s1 = A("st_s1", [128, 4], F32)
    st_s2 = A("st_s2", [128, 4], F32)
    st_mu = A("st_mu", [128, 4], F32)
    st_t = A("st_t", [128, 4], F32)
    st_var = A("st_var", [128, 4], F32)
    st_l = A("st_l", [128, 4], F32)
    st_r = A("st_r", [128, 4], F32)
    st_n = A("st_n", [128, 4], F32)

    sb_wp = A("sb_wp", [128, 4 * 1152], BF16)
    sb_wdt = A("sb_wdt", [128, 32], BF16)
    win_buf = A("win_buf", [128, 3 * 1024], BF16)
    wo_buf = A("wo_buf", [128, 2 * D], BF16)
    wg_buf = A("wg_buf", [128, 2 * 2048], BF16)
    wu_buf = A("wu_buf", [128, 2 * 2048], BF16)
    wd_buf = A("wd_buf", [128, 3 * 2048], BF16)
    gu = A("gu", [128, 4 * 512], BF16)

    sb_cst = A("sb_cst", [128, 6], F32)
    sb_cw = A("sb_cw", [128, 16], F32)
    sb_cb = A("sb_cb", [128, 4], F32)
    sb_dd = A("sb_dd", [128, 4], F32)
    sb_A = A("sb_A", [128, 4], F32)
    sb_bdt = A("sb_bdt", [32, 1], F32)
    sb_eb = A("sb_eb", [32, 512], F32)
    sb_id = A("sb_id", [128, 128], F32)

    assert nc.sbuf_bytes_remaining >= 0, nc.sbuf_bytes_remaining

    PS = nc.alloc_psum_tensor("PS", [128, 4096], F32)

    def pb(b, lo, hi):
        return PS[:, b * 512 + lo: b * 512 + hi]

    sch.sem("ld", "cw", "hs", "st", "wa", "wg", "wu", "wd", "pe", "ve", "se", "ge")
    bank_readers = {b: [] for b in range(8)}
    buf_readers = {}

    def pe_claim(banks):
        toks = []
        for b in banks:
            toks += bank_readers[b]
            bank_readers[b] = []
        sch.waits("tensor", toks)

    def read_bank(b, tok):
        bank_readers[b].append(tok)

    def buf_wait(eng, name):
        sch.waits(eng, buf_readers.get(name, ()))
        buf_readers[name] = []

    def buf_read(name, tok):
        buf_readers.setdefault(name, []).append(tok)

    if DBG_TAPS:
        g_bu = nc.dram_tensor("g_bu", [128, 1024], F32, kind="ExternalOutput").ap()
        g_state0 = nc.dram_tensor("g_state0", [128, 4], F32,
                                  kind="ExternalOutput").ap()
        g_cc = nc.dram_tensor("g_cc", [128, 1024], F32, kind="ExternalOutput").ap()

    streams = {}

    def stream_init(name, nslots, sem):
        streams[name] = dict(n=nslots, next=0, ready=[], sem=sem)

    def stream_load(name, fn, extra_waits=()):
        st = streams[name]
        slot = st["next"] % st["n"]
        buf_wait("sync", f"{name}{slot}")
        sch.waits("sync", extra_waits)
        tok = sch.do("sync", fn(slot), inc=st["sem"], n=16)
        st["ready"].append((slot, tok[1]))
        st["next"] += 1

    def stream_take(name):
        st = streams[name]
        return st["ready"].pop(0)

    stream_init("win", 3, "wa")
    stream_init("wo", 2, "wa")
    stream_init("wgb", 2, "wg")
    stream_init("wub", 2, "wu")
    stream_init("wdb", 3, "wd")

    for dst, src in ((sb_cw, d_cw), (sb_cb, d_cb), (sb_dd, d_dd), (sb_A, d_alog),
                     (sb_bdt, d_bdt), (sb_eb, d_eb), (sb_id, d_ident)):
        sch.do("sync", (lambda e, a=dst, b=src: e.dma_start(out=a[:, :], in_=b)),
               inc="ld", n=16)
    LD_CONST = sch.c["ld"]
    hs_ready = [None] * NT
    for tc in range(4):
        sch.do("sync", (lambda e, tc=tc: e.dma_start(
            out=sb_x[:, tc * D:(tc + 1) * D],
            in_=d_hs[tc * 128:(tc + 1) * 128, :])), inc="hs", n=16)
    hs_ready[0] = sch.c["hs"]

    sch.do("gpsimd", lambda e: e.dma_start(out=sb_wp[:, :], in_=d_wp), inc="ld", n=16)
    sch.do("gpsimd", lambda e: e.dma_start(out=sb_wdt[:, :], in_=d_wdt), inc="ld", n=16)
    LD_ALL = sch.c["ld"]
    cache_ready = {}
    sch.do("gpsimd", lambda e: e.dma_start(out=c_win, in_=d_win), inc="cw", n=16)
    cache_ready["win"] = sch.c["cw"]
    sch.do("gpsimd", lambda e: e.dma_start(out=c_wo, in_=d_wo), inc="cw", n=16)
    cache_ready["wo"] = sch.c["cw"]
    for j in range(4):
        sch.do("gpsimd", (lambda e, j=j: e.dma_start(
            out=c_wg[j * 16:(j + 1) * 16], in_=d_wg[j * 16:(j + 1) * 16])),
            inc="cw", n=16)
        cache_ready[("wg", j)] = sch.c["cw"]
        sch.do("gpsimd", (lambda e, j=j: e.dma_start(
            out=c_wu[j * 16:(j + 1) * 16], in_=d_wu[j * 16:(j + 1) * 16])),
            inc="cw", n=16)
        cache_ready[("wu", j)] = sch.c["cw"]
    for j in range(4):
        sch.do("gpsimd", (lambda e, j=j: e.dma_start(
            out=c_wd[j * 16:(j + 1) * 16], in_=d_wd[j * 16:(j + 1) * 16])),
            inc="cw", n=16)
        cache_ready[("wd", j)] = sch.c["cw"]

    for col, val in ((0, EPS), (1, -0.5), (2, -1.0), (3, 1.0), (4, 1.0 / D),
                     (5, 0.0)):
        sch.do("vector", (lambda e, col=col, val=val: e.memset(
            sb_cst[:, col:col + 1], val)), inc="ve")
    cst_done = sch.c["ve"]
    C_EPS = sb_cst[:, 0:1]
    C_NHALF = sb_cst[:, 1:2]
    C_NONE_ = sb_cst[:, 2:3]
    C_ONE = sb_cst[:, 3:4]
    C_INVD = sb_cst[:, 4:5]
    C_ZERO = sb_cst[:, 5:6]

    sch.wait("scalar", "ld", LD_CONST)
    t_exp = sch.do("scalar", lambda e: e.activation(sb_A[:, :], sb_A[:, :], AF.Exp),
                   inc="se")
    sch.wait("vector", "se", t_exp[1])
    sch.do("vector", lambda e: e.tensor_scalar_mul(sb_A[:, :], sb_A[:, :],
                                                   sb_cst[:, 2:3]), inc="ve")
    sch.wait("scalar", "ve", cst_done)

    def win_fn(k):
        def fn(slot):
            return lambda e: e.dma_start(
                out=win_buf[:, slot * 1024:(slot + 1) * 1024], in_=c_win[k])
        return fn

    def wo_fn(k):
        def fn(slot):
            return lambda e: e.dma_start(
                out=wo_buf[:, slot * D:(slot + 1) * D], in_=c_wo[k])
        return fn

    def wchunk_fn(cache, buf, hc):
        def fn(slot):
            return lambda e: e.dma_start(
                out=buf[:, slot * 2048:(slot + 1) * 2048], in_=cache[hc])
        return fn

    nt_build = DBG_NT if DBG_NT is not None else NT
    for ti in range(nt_build):
        seq_start = (ti % 4 == 0)

        sch.wait("tensor", "hs", hs_ready[ti])
        evB = [None] * NKD
        for kd in range(NKD):
            b = kd % 2
            pe_claim([b])
            last = None
            for tc in range(4):
                last = sch.do("tensor", (lambda e, b=b, tc=tc, kd=kd: e.transpose(
                    pb(b, tc * 128, (tc + 1) * 128),
                    sb_x[:, tc * D + kd * 128: tc * D + (kd + 1) * 128],
                    sb_id[:, :])),
                    inc="pe" if tc == 3 else None)
            eng, sem = ("vector", "ve") if kd % 2 == 0 else ("scalar", "se")
            sch.wait(eng, "pe", last[1])
            buf_wait(eng, f"fm{kd}")
            if eng == "vector":
                tok = sch.do(eng, (lambda e, b=b, kd=kd: e.tensor_copy(
                    sb_fm[:, kd * T:(kd + 1) * T], pb(b, 0, 512))), inc=sem)
            else:
                tok = sch.do(eng, (lambda e, b=b, kd=kd: e.copy(
                    sb_fm[:, kd * T:(kd + 1) * T], pb(b, 0, 512))), inc=sem)
            read_bank(b, tok)
            evB[kd] = tok

        if ti == 0:
            for k in range(3):
                stream_load("win", win_fn(k), [("cw", cache_ready["win"])])
        pe_xz_k = [None] * NKD
        for k in range(NKD):
            slot, rdy = stream_take("win")
            sch.wait("tensor", "wa", rdy)
            if k == 0:
                pe_claim(list(range(8)))
                sch.waits("tensor", [evB[14], evB[15]])
            last = None
            for m in range(8):
                last = sch.do("tensor", (lambda e, m=m, k=k, slot=slot: e.matmul(
                    pb(m, 0, 512),
                    win_buf[:, slot * 1024 + m * 128: slot * 1024 + (m + 1) * 128],
                    sb_fm[:, k * T:(k + 1) * T],
                    start=(k == 0), stop=(k == NKD - 1))),
                    inc="pe" if m == 7 else None)
            pe_xz_k[k] = last
            buf_read(f"win{slot}", last)
            nxt = k + 3
            if nxt < NKD:
                stream_load("win", win_fn(nxt))
        sch.wait("vector", "pe", pe_xz_k[NKD - 1][1])
        for c in range(4):
            xc = XC0 + c * (T + 3)
            tok = sch.do("vector", (lambda e, c=c, xc=xc: e.tensor_copy(
                scr[:, xc + 3: xc + 3 + T], pb(c, 0, 512))), inc="ve")
            read_bank(c, tok)
            if seq_start:
                sch.do("vector", (lambda e, xc=xc: e.memset(scr[:, xc:xc + 3], 0.0)),
                       inc="ve")
            else:
                sch.do("vector", (lambda e, c=c, xc=xc: e.tensor_copy(
                    scr[:, xc:xc + 3], sb_carry[:, 3 * c:3 * c + 3])), inc="ve")
        for c in range(4):
            xc = XC0 + c * (T + 3)
            xa = XACT0 + c * T
            sch.do("vector", (lambda e, c=c, xc=xc, xa=xa: e.tensor_scalar_mul(
                scr[:, xa:xa + T], scr[:, xc:xc + T], sb_cw[:, 4 * c:4 * c + 1])),
                inc="ve")
            for kk in range(1, 4):
                sch.do("vector", (lambda e, c=c, xc=xc, xa=xa, kk=kk:
                       e.scalar_tensor_tensor(
                           scr[:, xa:xa + T], scr[:, xc + kk:xc + kk + T],
                           sb_cw[:, 4 * c + kk:4 * c + kk + 1],
                           scr[:, xa:xa + T], ALU.mult, ALU.add)), inc="ve")
            cv = sch.do("vector", (lambda e, c=c, xc=xc: e.tensor_copy(
                sb_carry[:, 3 * c:3 * c + 3], scr[:, xc + T:xc + T + 3])), inc="ve")
        conv_done = cv
        sch.wait("scalar", "ve", conv_done[1])
        for c in range(4):
            xa = XACT0 + c * T
            sch.do("scalar", (lambda e, c=c, xa=xa: e.activation(
                scr[:, xa:xa + T], scr[:, xa:xa + T], AF.Silu,
                bias=sb_cb[:, c:c + 1])), inc="se")
        buf_wait("scalar", "gu")
        z_done = None
        for c in range(4):
            z_done = sch.do("scalar", (lambda e, c=c: e.activation(
                gu[:, c * 512:(c + 1) * 512], pb(4 + c, 0, 512), AF.Silu)),
                inc="se")
            read_bank(4 + c, z_done)
        xact_silu_done = sch.c["se"]
        sch.wait("vector", "se", xact_silu_done)
        buf_wait("vector", "xactbf")
        for c in range(4):
            xa = XACT0 + c * T
            cast_tok = sch.do("vector", (lambda e, c=c, xa=xa: e.tensor_copy(
                xactbf[:, c * T:(c + 1) * T], scr[:, xa:xa + T])), inc="ve")
        xactbf_ready = cast_tok

        def emit_sub_a(s):
            so = s * TA
            ev_par = [None] * 9
            for m in range(9):
                b = m % 2
                pe_claim([b])
                if m < 2 and s == 0:
                    sch.wait("tensor", "ve", xactbf_ready[1])
                last = None
                for kc in range(4):
                    last = sch.do("tensor", (lambda e, b=b, m=m, kc=kc, so=so:
                        e.matmul(
                            pb(b, 0, 256),
                            sb_wp[:, kc * 1152 + m * 128: kc * 1152 + (m + 1) * 128],
                            xactbf[:, kc * T + so: kc * T + so + TA],
                            start=(kc == 0), stop=(kc == 3))),
                        inc="pe" if kc == 3 else None)
                sch.wait("vector", "pe", last[1])
                if m == 0:
                    buf_wait("vector", "dtbf")
                    tok = sch.do("vector", (lambda e, b=b: e.tensor_copy(
                        dtbf[:, :], pb(b, 0, 256))), inc="ve")
                elif m <= 4:
                    cbu = BU0 + (m - 1) * TA
                    tok = sch.do("vector", (lambda e, b=b, cbu=cbu, m=m, so=so:
                           e.tensor_tensor(
                               scr[:, cbu:cbu + TA], pb(b, 0, 256),
                               scr[:, XACT0 + (m - 1) * T + so:
                                      XACT0 + (m - 1) * T + so + TA],
                               ALU.mult)), inc="ve")
                else:
                    ccc = CC0 + (m - 5) * TA
                    tok = sch.do("vector", (lambda e, b=b, ccc=ccc: e.tensor_copy(
                        scr[:, ccc:ccc + TA], pb(b, 0, 256))), inc="ve")
                read_bank(b, tok)
                ev_par[m] = tok
            pe_claim([2])
            sch.wait("tensor", "ve", ev_par[0][1])
            dmm = sch.do("tensor", lambda e: e.matmul(
                PS[0:32, 2 * 512: 2 * 512 + 256], sb_wdt[:, :], dtbf[:, :],
                start=True, stop=True), inc="pe")
            buf_read("dtbf", dmm)
            sch.wait("vector", "pe", dmm[1])
            buf_wait("vector", "dv")
            tok = sch.do("vector", lambda e: e.tensor_scalar_add(
                sb_dv[:, :], PS[0:32, 2 * 512: 2 * 512 + 256], sb_bdt[:, :]),
                inc="ve")
            read_bank(2, tok)
            dv_ready0 = sch.c["ve"]
            sch.wait("scalar", "ve", dv_ready0)
            buf_wait("scalar", "dt")
            abs_tok = sch.do("scalar", lambda e: e.activation(
                sb_da[:, :], sb_dv[:, :], AF.Abs), inc="se")
            buf_read("dv", abs_tok)
            sch.do("scalar", lambda e: e.activation(
                sb_dt[:, :], sb_da[:, :], AF.Exp, scale=sb_cst[0:32, 2:3]),
                inc="se")
            sch.do("scalar", lambda e: e.activation(
                sb_dt[:, :], sb_dt[:, :], AF.Ln, bias=sb_cst[0:32, 3:4]), inc="se")
            sp_done = sch.c["se"]
            sch.do("vector", lambda e: e.tensor_relu(sb_dm[:, :], sb_dv[:, :]),
                   inc="ve")
            sch.wait("vector", "se", sp_done)
            dv_tok = sch.do("vector", lambda e: e.tensor_tensor(
                sb_dv[:, :], sb_dm[:, :], sb_dt[:, :], ALU.add), inc="ve")
            buf_read("dt", dv_tok)
            dv_ready = dv_tok[1]
            for c in range(4):
                b = c % 2
                pe_claim([b])
                sch.wait("tensor", "ve", dv_ready)
                de = sch.do("tensor", (lambda e, b=b, c=c: e.matmul(
                    pb(b, 0, 256), sb_eb[:, c * 128:(c + 1) * 128], sb_dv[:, :],
                    start=True, stop=True)), inc="pe")
                if c == 3:
                    buf_read("dv", de)
                sch.wait("scalar", "pe", de[1])
                ab = sch.do("scalar", (lambda e, b=b, c=c: e.activation(
                    scr[:, ABAR0 + c * TA: ABAR0 + (c + 1) * TA], pb(b, 0, 256),
                    AF.Exp, scale=sb_A[:, c:c + 1])), inc="se")
                sch.wait("vector", "se", ab[1])
                bu = sch.do("vector", (lambda e, b=b, c=c: e.tensor_tensor(
                    scr[:, BU0 + c * TA: BU0 + (c + 1) * TA], pb(b, 0, 256),
                    scr[:, BU0 + c * TA: BU0 + (c + 1) * TA], ALU.mult)), inc="ve")
                read_bank(b, bu)

        def emit_sub_b(s):
            so = s * TA
            for c in range(4):
                if seq_start and s == 0:
                    sch.do("vector", (lambda e, c=c: e.tensor_tensor_scan(
                        scr[:, HH0 + c * TA: HH0 + (c + 1) * TA],
                        scr[:, ABAR0 + c * TA: ABAR0 + (c + 1) * TA],
                        scr[:, BU0 + c * TA: BU0 + (c + 1) * TA],
                        sb_cst[:, 5:6], ALU.mult, ALU.add)), inc="ve")
                else:
                    sch.do("vector", (lambda e, c=c: e.tensor_tensor_scan(
                        scr[:, HH0 + c * TA: HH0 + (c + 1) * TA],
                        scr[:, ABAR0 + c * TA: ABAR0 + (c + 1) * TA],
                        scr[:, BU0 + c * TA: BU0 + (c + 1) * TA],
                        sb_state[:, c:c + 1], ALU.mult, ALU.add)), inc="ve")
            sch.wait("scalar", "ve", sch.c["ve"])
            sbn = sch.do("scalar", lambda e: e.nop(), inc="se")
            sch.wait("vector", "se", sbn[1])
            for c in range(4):
                sch.do("vector", (lambda e, c=c: e.tensor_copy(
                    sb_state[:, c:c + 1],
                    scr[:, HH0 + c * TA + TA - 1: HH0 + c * TA + TA])), inc="ve")
            ytok = None
            for c in range(4):
                ccc = CC0 + c * TA
                sch.do("vector", (lambda e, c=c, ccc=ccc: e.tensor_tensor(
                    scr[:, ccc:ccc + TA], scr[:, ccc:ccc + TA],
                    scr[:, HH0 + c * TA: HH0 + (c + 1) * TA], ALU.mult)), inc="ve")
                sch.do("vector", (lambda e, c=c, ccc=ccc, so=so:
                       e.scalar_tensor_tensor(
                           scr[:, ccc:ccc + TA],
                           scr[:, XACT0 + c * T + so: XACT0 + c * T + so + TA],
                           sb_dd[:, c:c + 1], scr[:, ccc:ccc + TA],
                           ALU.mult, ALU.add)), inc="ve")
                if c == 0 and s == 0:
                    sch.wait("vector", "se", z_done[1])
                gtok = sch.do("vector", (lambda e, c=c, ccc=ccc, so=so:
                       e.tensor_tensor(
                           scr[:, ccc:ccc + TA],
                           gu[:, c * 512 + so: c * 512 + so + TA],
                           scr[:, ccc:ccc + TA], ALU.mult)), inc="ve")
                if s == 1:
                    buf_read("gu", gtok)
                ytok = sch.do("vector", (lambda e, c=c, ccc=ccc, so=so:
                    e.tensor_copy(
                        ybf[:, c * T + so: c * T + so + TA],
                        scr[:, ccc:ccc + TA])), inc="ve")
            if DBG_TAPS and ti == 0:
                if s == 0:
                    sch.wait("sync", "ve", sch.c["ve"])
                    tapt = sch.do("sync", (lambda e: e.dma_start(
                        out=g_state0, in_=sb_state[:, :])), inc="st", n=16)
                    sch.wait("vector", "st", tapt[1])
                else:
                    sch.wait("sync", "ve", sch.c["ve"])
                    tapt = sch.do("sync", (lambda e: e.dma_start(
                        out=g_bu, in_=scr[:, BU0:BU0 + 1024])), inc="st", n=16)
                    tapt2 = sch.do("sync", (lambda e: e.dma_start(
                        out=g_cc, in_=scr[:, CC0:CC0 + 1024])), inc="st", n=16)
                    sch.wait("vector", "st", tapt2[1])
            return ytok

        def emit_J_half(half, ybf_tok):
            tcs = (0, 1) if half == 0 else (2, 3)
            last = None
            for k in range(4):
                slot, rdy = stream_take("wo")
                sch.wait("tensor", "wa", rdy)
                if k == 0:
                    pe_claim(list(range(8)))
                    sch.wait("tensor", "ve", ybf_tok[1])
                for tc in tcs:
                    tcb = 0 if tc % 2 == 0 else 4
                    for n4 in range(4):
                        last = sch.do("tensor", (lambda e, tcb=tcb, n4=n4, k=k,
                                      slot=slot, tc=tc: e.matmul(
                            pb(tcb + n4, 0, 512),
                            ybf[:, k * T + tc * 128: k * T + (tc + 1) * 128],
                            wo_buf[:, slot * D + n4 * 512:
                                   slot * D + (n4 + 1) * 512],
                            start=(k == 0), stop=(k == 3))),
                            inc="pe" if (tc == tcs[1] and n4 == 3) else None)
                buf_read(f"wo{slot}", last)
                nk = k + 2
                if nk < 4:
                    stream_load("wo", wo_fn(nk))
                elif half == 0:
                    stream_load("wo", wo_fn(nk - 4))
            sch.wait("vector", "pe", last[1])
            for tc in tcs:
                tcb = 0 if tc % 2 == 0 else 4
                rtok = sch.do("vector", (lambda e, tc=tc, tcb=tcb: e.tensor_tensor(
                    sb_x[:, tc * D:(tc + 1) * D], PS[:, tcb * 512: tcb * 512 + D],
                    sb_x[:, tc * D:(tc + 1) * D], ALU.add)), inc="ve")
                for b in range(tcb, tcb + 4):
                    read_bank(b, rtok)
            for tc in tcs:
                sch.do("vector", (lambda e, tc=tc: e.reduce_sum(
                    st_s1[:, tc:tc + 1], sb_x[:, tc * D:(tc + 1) * D], AX.X)),
                    inc="ve")
                dtok = sch.do("vector", (lambda e, tc=tc: e.scalar_tensor_tensor(
                    sb_gated[:, 0:2048], sb_x[:, tc * D:(tc + 1) * D],
                    sb_cst[:, 3:4],
                    sb_x[:, tc * D:(tc + 1) * D], ALU.mult, ALU.mult,
                    accum_out=st_s2[:, tc:tc + 1])), inc="ve")
                buf_read("gatedump", dtok)
            sch.wait("scalar", "ve", sch.c["ve"])
            lbn = sch.do("scalar", lambda e: e.nop(), inc="se")
            sch.wait("vector", "se", lbn[1])
            for tc in tcs:
                sch.do("vector", (lambda e, tc=tc: e.tensor_scalar_mul(
                    st_mu[:, tc:tc + 1], st_s1[:, tc:tc + 1], sb_cst[:, 4:5])),
                    inc="ve")
            for tc in tcs:
                sch.do("vector", (lambda e, tc=tc: e.tensor_tensor(
                    st_t[:, tc:tc + 1], st_mu[:, tc:tc + 1], st_mu[:, tc:tc + 1],
                    ALU.mult)), inc="ve")
            vtok = None
            for tc in tcs:
                vtok = sch.do("vector", (lambda e, tc=tc: e.scalar_tensor_tensor(
                    st_var[:, tc:tc + 1], st_s2[:, tc:tc + 1], sb_cst[:, 4:5],
                    st_t[:, tc:tc + 1], ALU.mult, ALU.subtract)), inc="ve")
            sch.wait("scalar", "ve", vtok[1])
            rstok = None
            for tc in tcs:
                sch.do("scalar", (lambda e, tc=tc: e.activation(
                    st_l[:, tc:tc + 1], st_var[:, tc:tc + 1], AF.Ln,
                    bias=sb_cst[:, 0:1])), inc="se")
                rstok = sch.do("scalar", (lambda e, tc=tc: e.activation(
                    st_r[:, tc:tc + 1], st_l[:, tc:tc + 1], AF.Exp,
                    scale=sb_cst[:, 1:2])), inc="se")
            sch.wait("vector", "se", rstok[1])
            for tc in tcs:
                sch.do("vector", (lambda e, tc=tc: e.scalar_tensor_tensor(
                    st_n[:, tc:tc + 1], st_mu[:, tc:tc + 1], sb_cst[:, 2:3],
                    st_r[:, tc:tc + 1], ALU.mult, ALU.mult)), inc="ve")
            for tc in tcs:
                ln1_done[tc] = sch.do("vector", (lambda e, tc=tc: e.tensor_scalar(
                    sb_x[:, tc * D:(tc + 1) * D], sb_x[:, tc * D:(tc + 1) * D],
                    st_r[:, tc:tc + 1], st_n[:, tc:tc + 1], ALU.mult, ALU.add)),
                    inc="ve")

        def emit_L_pair(tcs):
            for tc in tcs:
                sch.wait("tensor", "ve", ln1_done[tc][1])
                for kq in range(4):
                    b = kq % 2
                    pe_claim([b])
                    last = None
                    for j in range(4):
                        kd = kq * 4 + j
                        last = sch.do("tensor", (lambda e, b=b, j=j, tc=tc, kd=kd:
                            e.transpose(
                                pb(b, j * 128, (j + 1) * 128),
                                sb_x[:, tc * D + kd * 128: tc * D + (kd + 1) * 128],
                                sb_id[:, :])),
                            inc="pe" if j == 3 else None)
                    eng, sem = ("vector", "ve") if kq % 2 == 0 else ("scalar", "se")
                    sch.wait(eng, "pe", last[1])
                    fmv = sb_fm.ap().rearrange("p (k t) -> p k t", t=T)[
                        :, kq * 4:(kq + 1) * 4, tc * 128:(tc + 1) * 128]
                    psv = PS[:, b * 512:(b + 1) * 512].rearrange(
                        "p (j x) -> p j x", x=128)
                    if eng == "vector":
                        tok = sch.do(eng, (lambda e, fmv=fmv, psv=psv:
                            e.tensor_copy(fmv, psv)), inc=sem)
                    else:
                        tok = sch.do(eng, (lambda e, fmv=fmv, psv=psv:
                            e.copy(fmv, psv)), inc=sem)
                    read_bank(b, tok)
                    evL.append(tok)

        ln1_done = [None] * 4
        evL = []
        if ti == 0:
            for k in range(2):
                stream_load("wo", wo_fn(k), [("cw", cache_ready["wo"])])
        emit_sub_a(0)
        ybf0 = emit_sub_b(0)
        emit_sub_a(1)
        ybf1 = emit_sub_b(1)
        emit_J_half(0, ybf0)
        emit_J_half(1, ybf1)
        emit_L_pair((0, 1))
        emit_L_pair((2, 3))

        if ti == 0:
            stream_load("wgb", wchunk_fn(c_wg, wg_buf, 0),
                        [("cw", cache_ready[("wg", 0)])])
            stream_load("wub", wchunk_fn(c_wu, wu_buf, 0),
                        [("cw", cache_ready[("wu", 0)])])
            stream_load("wgb", wchunk_fn(c_wg, wg_buf, 1),
                        [("cw", cache_ready[("wg", 0)])])
            stream_load("wub", wchunk_fn(c_wu, wu_buf, 1),
                        [("cw", cache_ready[("wu", 0)])])
        buf_wait("gpsimd", "gated")
        buf_wait("gpsimd", "gatedump")
        gated_done = None
        for hc in range(NHC):
            gb = 0 + 2 * (hc % 2)
            ub = 1 + 2 * (hc % 2)
            gslot, grdy = stream_take("wgb")
            uslot, urdy = stream_take("wub")
            sch.wait("tensor", "wg", grdy)
            sch.wait("tensor", "wu", urdy)
            pe_claim([gb])
            if hc == 0:
                sch.waits("tensor", [evL[-2], evL[-1]])
            lastg = None
            for k in range(NKD):
                lastg = sch.do("tensor", (lambda e, gb=gb, k=k, gslot=gslot: e.matmul(
                    pb(gb, 0, 512),
                    wg_buf[:, gslot * 2048 + k * 128: gslot * 2048 + (k + 1) * 128],
                    sb_fm[:, k * T:(k + 1) * T],
                    start=(k == 0), stop=(k == NKD - 1))),
                    inc="pe" if k == NKD - 1 else None)
            buf_read(f"wgb{gslot}", lastg)
            pe_claim([ub])
            lastu = None
            for k in range(NKD):
                lastu = sch.do("tensor", (lambda e, ub=ub, k=k, uslot=uslot: e.matmul(
                    pb(ub, 0, 512),
                    wu_buf[:, uslot * 2048 + k * 128: uslot * 2048 + (k + 1) * 128],
                    sb_fm[:, k * T:(k + 1) * T],
                    start=(k == 0), stop=(k == NKD - 1))),
                    inc="pe" if k == NKD - 1 else None)
            buf_read(f"wub{uslot}", lastu)
            if hc == NHC - 1:
                buf_read("fm_rd", lastu)
            nhc = hc + 2
            if nhc < NHC:
                stream_load("wgb", wchunk_fn(c_wg, wg_buf, nhc),
                            [("cw", cache_ready[("wg", nhc // 16)])] if ti == 0 else ())
                stream_load("wub", wchunk_fn(c_wu, wu_buf, nhc),
                            [("cw", cache_ready[("wu", nhc // 16)])] if ti == 0 else ())
            sch.wait("scalar", "pe", lastg[1])
            buf_wait("scalar", "gu")
            stok = sch.do("scalar", (lambda e, gb=gb: e.activation(
                gu[:, 0:512], pb(gb, 0, 512), AF.Silu)), inc="se")
            read_bank(gb, stok)
            sch.wait("scalar", "pe", lastu[1])
            utok = sch.do("scalar", (lambda e, ub=ub: e.copy(
                gu[:, 512:1024], pb(ub, 0, 512))), inc="se")
            read_bank(ub, utok)
            sch.wait("gpsimd", "se", utok[1])
            gated_done = sch.do("gpsimd", (lambda e, hc=hc: e.tensor_tensor(
                sb_gated[:, hc * T:(hc + 1) * T], gu[:, 0:512],
                gu[:, 512:1024], ALU.mult)), inc="ge")
            buf_read("gu", gated_done)

        if ti == 0:
            for hcn in range(3):
                stream_load("wdb", wchunk_fn(c_wd, wd_buf, hcn),
                            [("cw", cache_ready[("wd", 0)])])
        st_count_after = [None] * 4
        for half in range(2):
            tcs = (0, 1) if half == 0 else (2, 3)
            last = None
            for hc in range(NHC):
                slot, rdy = stream_take("wdb")
                sch.wait("tensor", "wd", rdy)
                if hc == 0:
                    pe_claim(list(range(8)))
                    sch.wait("tensor", "ge", gated_done[1])
                for tc in tcs:
                    tcb = 0 if tc % 2 == 0 else 4
                    for n4 in range(4):
                        last = sch.do("tensor", (lambda e, tcb=tcb, n4=n4, hc=hc,
                                      slot=slot, tc=tc: e.matmul(
                            pb(tcb + n4, 0, 512),
                            sb_gated[:, hc * T + tc * 128: hc * T + (tc + 1) * 128],
                            wd_buf[:, slot * 2048 + n4 * 512:
                                   slot * 2048 + (n4 + 1) * 512],
                            start=(hc == 0), stop=(hc == NHC - 1))),
                            inc="pe" if (tc == tcs[1] and n4 == 3) else None)
                buf_read(f"wdb{slot}", last)
                if hc == NHC - 1 and half == 1:
                    buf_read("gated", last)
                nhc = hc + 3
                if nhc < NHC:
                    stream_load("wdb", wchunk_fn(c_wd, wd_buf, nhc),
                                [("cw", cache_ready[("wd", nhc // 16)])]
                                if ti == 0 else ())
                elif half == 0:
                    stream_load("wdb", wchunk_fn(c_wd, wd_buf, nhc - NHC),
                                [("cw", cache_ready[("wd", (nhc - NHC) // 16)])]
                                if ti == 0 else ())
            sch.wait("vector", "pe", last[1])
            for tc in tcs:
                tcb = 0 if tc % 2 == 0 else 4
                rtok = sch.do("vector", (lambda e, tc=tc, tcb=tcb: e.tensor_tensor(
                    sb_x[:, tc * D:(tc + 1) * D], PS[:, tcb * 512: tcb * 512 + D],
                    sb_x[:, tc * D:(tc + 1) * D], ALU.add)), inc="ve")
                for b in range(tcb, tcb + 4):
                    read_bank(b, rtok)
            for tc in tcs:
                sch.do("vector", (lambda e, tc=tc: e.reduce_sum(
                    st_s1[:, tc:tc + 1], sb_x[:, tc * D:(tc + 1) * D], AX.X)),
                    inc="ve")
                sch.do("vector", (lambda e, tc=tc: e.scalar_tensor_tensor(
                    scr[:, DUMP[0]:DUMP[1]], sb_x[:, tc * D:(tc + 1) * D],
                    sb_cst[:, 3:4],
                    sb_x[:, tc * D:(tc + 1) * D], ALU.mult, ALU.mult,
                    accum_out=st_s2[:, tc:tc + 1])), inc="ve")
            sch.wait("scalar", "ve", sch.c["ve"])
            nbn = sch.do("scalar", lambda e: e.nop(), inc="se")
            sch.wait("vector", "se", nbn[1])
            for tc in tcs:
                sch.do("vector", (lambda e, tc=tc: e.tensor_scalar_mul(
                    st_mu[:, tc:tc + 1], st_s1[:, tc:tc + 1], sb_cst[:, 4:5])),
                    inc="ve")
            for tc in tcs:
                sch.do("vector", (lambda e, tc=tc: e.tensor_tensor(
                    st_t[:, tc:tc + 1], st_mu[:, tc:tc + 1], st_mu[:, tc:tc + 1],
                    ALU.mult)), inc="ve")
            vtok = None
            for tc in tcs:
                vtok = sch.do("vector", (lambda e, tc=tc: e.scalar_tensor_tensor(
                    st_var[:, tc:tc + 1], st_s2[:, tc:tc + 1], sb_cst[:, 4:5],
                    st_t[:, tc:tc + 1], ALU.mult, ALU.subtract)), inc="ve")
            sch.wait("scalar", "ve", vtok[1])
            rstok = None
            for tc in tcs:
                sch.do("scalar", (lambda e, tc=tc: e.activation(
                    st_l[:, tc:tc + 1], st_var[:, tc:tc + 1], AF.Ln,
                    bias=sb_cst[:, 0:1])), inc="se")
                rstok = sch.do("scalar", (lambda e, tc=tc: e.activation(
                    st_r[:, tc:tc + 1], st_l[:, tc:tc + 1], AF.Exp,
                    scale=sb_cst[:, 1:2])), inc="se")
            sch.wait("vector", "se", rstok[1])
            for tc in tcs:
                sch.do("vector", (lambda e, tc=tc: e.scalar_tensor_tensor(
                    st_n[:, tc:tc + 1], st_mu[:, tc:tc + 1], sb_cst[:, 2:3],
                    st_r[:, tc:tc + 1], ALU.mult, ALU.mult)), inc="ve")
            for tc in tcs:
                x2tok = sch.do("vector", (lambda e, tc=tc: e.tensor_scalar(
                    sb_x[:, tc * D:(tc + 1) * D], sb_x[:, tc * D:(tc + 1) * D],
                    st_r[:, tc:tc + 1], st_n[:, tc:tc + 1], ALU.mult, ALU.add)),
                    inc="ve")
                sch.wait("scalar", "ve", x2tok[1])
                stok = sch.do("scalar", (lambda e, ti=ti, tc=tc: e.dma_start(
                    out=d_out[ti * T + tc * 128: ti * T + (tc + 1) * 128, :],
                    in_=sb_x[:, tc * D:(tc + 1) * D])), inc="st", n=16)
                st_count_after[tc] = stok[1]
        if ti + 1 < NT:
            for tc in range(4):
                sch.wait("sync", "st", st_count_after[tc])
                sch.do("sync", (lambda e, ti=ti, tc=tc: e.dma_start(
                    out=sb_x[:, tc * D:(tc + 1) * D],
                    in_=d_hs[(ti + 1) * T + tc * 128: (ti + 1) * T + (tc + 1) * 128,
                             :])), inc="hs", n=16)
            hs_ready[ti + 1] = sch.c["hs"]
            for k in range(3):
                stream_load("win", win_fn(k))
            for k in range(2):
                stream_load("wo", wo_fn(k))
            for hcn in range(2):
                stream_load("wgb", wchunk_fn(c_wg, wg_buf, hcn))
                stream_load("wub", wchunk_fn(c_wu, wu_buf, hcn))
            for hcn in range(3):
                stream_load("wdb", wchunk_fn(c_wd, wd_buf, hcn))

    if DBG_TAPS:
        g_fm = nc.dram_tensor("g_fm", [128, NKD * T], BF16, kind="ExternalOutput").ap()
        g_xact = nc.dram_tensor("g_xact", [128, 4 * T], F32, kind="ExternalOutput").ap()
        g_dv = nc.dram_tensor("g_dv", [32, TA], F32, kind="ExternalOutput").ap()
        g_abar = nc.dram_tensor("g_abar", [128, 1024], F32, kind="ExternalOutput").ap()
        g_hh = nc.dram_tensor("g_hh", [128, 1024], F32, kind="ExternalOutput").ap()
        g_ybf = nc.dram_tensor("g_ybf", [128, 4 * T], BF16, kind="ExternalOutput").ap()
        g_gated = nc.dram_tensor("g_gated", [128, NHC * T], BF16,
                                 kind="ExternalOutput").ap()
        g_state = nc.dram_tensor("g_state", [128, 4], F32, kind="ExternalOutput").ap()
        for sem in ("pe", "ve", "se", "ge"):
            sch.wait("sync", sem, sch.c[sem])
        for dst, srcb in ((g_fm, sb_fm), (g_xact, None), (g_dv, sb_dv),
                          (g_abar, None), (g_hh, None), (g_ybf, ybf),
                          (g_gated, sb_gated), (g_state, sb_state)):
            if dst is g_xact:
                sch.do("sync", (lambda e: e.dma_start(
                    out=g_xact, in_=scr[:, XACT0:XACT0 + 4 * T])), inc="st", n=16)
            elif dst is g_abar:
                sch.do("sync", (lambda e: e.dma_start(
                    out=g_abar, in_=scr[:, ABAR0:ABAR0 + 1024])), inc="st", n=16)
            elif dst is g_hh:
                sch.do("sync", (lambda e: e.dma_start(
                    out=g_hh, in_=scr[:, HH0:HH0 + 1024])), inc="st", n=16)
            else:
                sch.do("sync", (lambda e, dst=dst, srcb=srcb: e.dma_start(
                    out=dst, in_=srcb[:, :])), inc="st", n=16)

    sch.wait("sync", "st", sch.c["st"])
    return nc


def _prep(inputs):
    hs = np.asarray(inputs["L0_attn_hidden_states"])
    wx = np.asarray(inputs["L0_attn_W_in_x"])
    wz = np.asarray(inputs["L0_attn_W_in_z"])
    convw = np.asarray(inputs["L0_attn_conv_w"])
    convb = np.asarray(inputs["L0_attn_conv_b"])
    wparam = np.asarray(inputs["L0_attn_W_param"])
    wdt = np.asarray(inputs["L0_attn_W_dt"])
    bdt = np.asarray(inputs["L0_attn_b_dt"])
    alog = np.asarray(inputs["L0_attn_A_log"])
    dd = np.asarray(inputs["L0_attn_D"])
    wout = np.asarray(inputs["L0_attn_W_out"])
    wg = np.asarray(inputs["L0_ffn_w_gate"])
    wu = np.asarray(inputs["L0_ffn_w_up"])
    wd = np.asarray(inputs["L0_ffn_w_down"])

    C = np.ascontiguousarray
    win = C(np.concatenate(
        [wx.T.reshape(NKD, 128, DI), wz.T.reshape(NKD, 128, DI)], axis=2))
    wp = C(wparam.T.reshape(4, 128, 1152).transpose(1, 0, 2).reshape(128, 4 * 1152))
    wdt_t = C(wdt.T)
    wo = C(wout.T.reshape(4, 128, D))
    wg_t = C(wg.T.reshape(NKD, 128, NHC, 128).transpose(2, 1, 0, 3)
             .reshape(NHC, 128, 2048))
    wu_t = C(wu.T.reshape(NKD, 128, NHC, 128).transpose(2, 1, 0, 3)
             .reshape(NHC, 128, 2048))
    wd_t = C(wd.T.reshape(NHC, 128, D))
    cw = C(convw.reshape(4, 128, 4).transpose(1, 0, 2).reshape(128, 16))
    cb = C(convb.reshape(4, 128).T)
    dd_t = C(dd.reshape(4, 128).T)
    alog_t = C(alog.reshape(4, 128).T)
    bdt_t = C(bdt.reshape(32, 1))
    eb = np.zeros((32, 512), np.float32)
    for c in range(4):
        for p in range(128):
            eb[8 * c + p // 16, c * 128 + p] = 1.0
    ident = np.eye(128, dtype=np.float32)

    shared = dict(win=win, wp=wp, wdt=wdt_t, wo=wo, wg=wg_t, wu=wu_t, wd=wd_t,
                  cw=cw, cb=cb, dd=dd_t, alog=alog_t, bdt=bdt_t, eb=eb,
                  ident=ident)
    in_maps = []
    hs4 = hs.reshape(NCORE, 2 * 2048, D)
    for core in range(NCORE):
        m = dict(shared)
        m["hs"] = C(hs4[core])
        in_maps.append(m)
    return in_maps


def kernel(**inputs):
    sch = Sched()
    nc = build(sch)

    with contextlib.ExitStack() as stack:
        S = {}
        for name in sch.c:
            S[name] = stack.enter_context(nc.semaphore(name))

        def replay(eng_name):
            def f(e):
                for item in sch.q[eng_name]:
                    if item[0] == "w":
                        e.wait_ge(S[item[1]], item[2])
                    else:
                        ins = item[1](e)
                        if item[2] is not None:
                            ins.then_inc(S[item[2]], item[3])
            return f

        with nc.Block() as block:
            block.tensor(replay("tensor"))
            block.vector(replay("vector"))
            block.scalar(replay("scalar"))
            block.gpsimd(replay("gpsimd"))
            block.sync(replay("sync"))

    in_maps = _prep(inputs)
    res = run_bass_kernel_spmd(nc, in_maps, core_ids=list(range(NCORE)))
    out = np.concatenate(
        [np.asarray(r["out"], dtype=np.float32).reshape(2, 2048, D)
         for r in res.results], axis=0)
    return out


def _debug_run(inputs, nt=1):
    global DBG_NT, DBG_TAPS
    DBG_NT, DBG_TAPS = nt, True
    try:
        sch = Sched()
        nc = build(sch)
        with contextlib.ExitStack() as stack:
            S = {}
            for name in sch.c:
                S[name] = stack.enter_context(nc.semaphore(name))

            def replay(eng_name):
                def f(e):
                    for item in sch.q[eng_name]:
                        if item[0] == "w":
                            e.wait_ge(S[item[1]], item[2])
                        else:
                            ins = item[1](e)
                            if item[2] is not None:
                                ins.then_inc(S[item[2]], item[3])
                return f

            with nc.Block() as block:
                block.tensor(replay("tensor"))
                block.vector(replay("vector"))
                block.scalar(replay("scalar"))
                block.gpsimd(replay("gpsimd"))
                block.sync(replay("sync"))
        in_maps = _prep(inputs)
        res = run_bass_kernel_spmd(nc, in_maps[:1], core_ids=[0])
        return res.results[0]
    finally:
        DBG_NT, DBG_TAPS = None, False
